# version 01 (final)
# baseline (speedup 1.0000x reference)
import contextlib
import math
import numpy as np
import ml_dtypes
import concourse.bass as bass
import concourse.mybir as mybir
from concourse.bass_utils import run_bass_kernel_spmd

F32 = mybir.dt.float32
BF16 = mybir.dt.bfloat16
F8 = mybir.dt.float8e4
AF = mybir.ActivationFunctionType
ALU = mybir.AluOpType
bf16 = ml_dtypes.bfloat16

C = 2048
T = 2048
B = 4
TL = 1024
NCT = 16
H, KV, D = 16, 4, 128
DIN = 4096
NH, HD, DS = 64, 64, 128
CHUNK = 128
XBC = DIN + 2 * DS
I0 = 6144
H1 = 5632
NI0 = I0 // 128
NDIN = DIN // 128
NXBC = XBC // 128
NH1 = H1 // 128
LN_EPS = 1e-5
RMS_EPS = 1e-5
BIT_EPS = 1e-8
MAGIC = float(np.float32(1.5 * 2.0 ** 23))



def _slabc(w):
    cin, cout = w.shape
    nci, ncf = cin // 128, cout // 128
    return np.ascontiguousarray(
        w.reshape(nci, 128, ncf, 128).transpose(1, 2, 0, 3).reshape(128, -1))


def _cols(v):
    n = v.shape[0] // 128
    return np.ascontiguousarray(v.reshape(n, 128).T)


def _rope_tables():
    inv = 1.0 / 10000.0 ** (np.arange(0, D, 2, dtype=np.float64) / D)
    f = np.arange(T, dtype=np.float64)[:, None] * inv[None, :]
    c = np.concatenate([np.cos(f)] * 2, axis=1)
    s = np.concatenate([np.sin(f)] * 2, axis=1)
    return (np.ascontiguousarray(c.T.astype(np.float32)),
            np.ascontiguousarray(s.T.astype(np.float32)))


def _attn_masks():
    k = np.arange(128)[:, None]
    j = np.arange(512 + 384)[None, :]
    return np.ascontiguousarray((j >= k + 384).astype(np.float32))



class Sem:
    def __init__(self, handle):
        self.h = handle
        self.n = 0

    def inc(self, instr, k=1):
        instr.then_inc(self.h, k)
        self.n += k
        return self.n

    def mark(self):
        return (self, self.n)


def wait(eng, mark):
    if mark is not None and mark[1] > 0:
        eng.wait_ge(mark[0].h, mark[1])


class Prog:
    def __init__(self):
        self.nc = bass.Bass("TRN2", target_bir_lowering=False)
        self.stack = contextlib.ExitStack()
        self.block = None

    def dram_in(self, name, shape, dtype=F32):
        return self.nc.dram_tensor(name, list(shape), dtype,
                                   kind="ExternalInput").ap()

    def dram_out(self, name, shape, dtype=F32):
        return self.nc.dram_tensor(name, list(shape), dtype,
                                   kind="ExternalOutput").ap()

    def sb(self, name, shape, dtype):
        return self.stack.enter_context(
            self.nc.sbuf_tensor(name, list(shape), dtype))

    def ps(self, name, shape, dtype=F32):
        return self.stack.enter_context(
            self.nc.psum_tensor(name, list(shape), dtype))

    def sem(self, name):
        return Sem(self.stack.enter_context(self.nc.semaphore(name)))

    def begin(self):
        self.block = self.stack.enter_context(self.nc.Block())

    def T(self, fn): self.block.tensor(fn)
    def V(self, fn): self.block.vector(fn)
    def S(self, fn): self.block.scalar(fn)
    def G(self, fn): self.block.gpsimd(fn)
    def Y(self, fn): self.block.sync(fn)

    def close(self):
        self.stack.close()


class WStream:

    def __init__(self, p, name, ld, nci, dtype=BF16, nbuf=2):
        self.p, self.ld = p, ld
        self.bufs = [p.sb(f"{name}_b{i}", (128, nci * 128), dtype)
                     for i in range(nbuf)]
        self.nbuf = nbuf
        self.seq = 0
        self.ld_val = {}
        self.done = {}

    def load(self, dram, chunk_j, nci):
        i = self.seq
        buf = self.bufs[i % self.nbuf]
        w = nci * 128

        def _(g):
            if i >= self.nbuf:
                wait(g, self.done[i - self.nbuf])
            self.ld.inc(g.dma_start(
                out=buf[:, 0:w], in_=dram[:, chunk_j * w:(chunk_j + 1) * w]), 16)
        self.p.G(_)
        self.ld_val[i] = self.ld.n
        self.seq += 1
        return buf, i

    def mark_done(self, mark):
        self.done[self.seq - 1] = mark



def ln_emit(p, sems, y_tiles, write_out, bufs, tb=512, d_tiles=16, dim=None,
            ready=None):
    st, sv, ss = sems
    dim = dim or d_tiles * 128
    onec, oner = bufs["ones_col"], bufs["ones_row"]
    onecb = bufs["ones_colb"]
    sq = bufs["sq_scratch"]
    bf = bufs.setdefault("free", {})

    def _(t):
        wait(t, bf.get("stats"))
        wait(t, ready)
        for i in range(d_tiles):
            t.matmul(bufs["ps_sum"][:, 0:tb], onecb[:], y_tiles[i],
                     start=(i == 0), stop=(i == d_tiles - 1))
    p.T(_)

    sq_used = {}
    for i in range(d_tiles):
        def _(sc, i=i):
            if i == 0:
                wait(sc, ready)
            if i >= 1:
                wait(sc, sq_used[i - 1])
            ss.inc(sc.activation(out=sq[i % 2][:, 0:tb], in_=y_tiles[i],
                                 func=AF.Square))
        p.S(_)
        ss_sq = ss.n

        def _(t, i=i, ss_sq=ss_sq):
            t.wait_ge(ss.h, ss_sq)
            st.inc(t.matmul(bufs["ps_sumsq"][:, 0:tb], onec[:],
                            sq[i % 2][:, 0:tb], start=(i == 0),
                            stop=(i == d_tiles - 1)))
        p.T(_)
        sq_used[i] = st.mark()
    t_stats = st.n

    def _(v):
        v.wait_ge(st.h, t_stats)
        v.tensor_scalar(out=bufs["st_mean"][:, 0:tb],
                        in0=bufs["ps_sum"][:, 0:tb],
                        scalar1=1.0 / dim, scalar2=None, op0=ALU.mult)
        sv.inc(v.tensor_scalar(out=bufs["st_var"][:, 0:tb],
                               in0=bufs["ps_sumsq"][:, 0:tb],
                               scalar1=1.0 / dim, scalar2=None, op0=ALU.mult))
        v.wait_ge(sv.h, sv.n)
        sv.inc(v.tensor_tensor(out=bufs["st_b"][:, 0:tb],
                               in0=bufs["st_mean"][:, 0:tb],
                               in1=bufs["st_mean"][:, 0:tb], op=ALU.mult))
        v.wait_ge(sv.h, sv.n)
        sv.inc(v.tensor_tensor(out=bufs["st_var"][:, 0:tb],
                               in0=bufs["st_var"][:, 0:tb],
                               in1=bufs["st_b"][:, 0:tb], op=ALU.subtract))
    p.V(_)
    bf["stats"] = sv.mark()
    v_var = sv.n

    def _(v):
        v.wait_ge(sv.h, v_var)
        sv.inc(v.tensor_scalar(out=bufs["st_var"][:, 0:tb],
                               in0=bufs["st_var"][:, 0:tb], scalar1=LN_EPS,
                               scalar2=None, op0=ALU.add))
    p.V(_)
    v_vareps = sv.n

    def _(sc):
        sc.wait_ge(sv.h, v_vareps)
        sc.activation(out=sq[0][0:1, 0:tb], in_=bufs["st_var"][:, 0:tb],
                      func=AF.Ln)
        ss.inc(sc.activation(out=bufs["st_rstd"][:, 0:tb],
                             in_=sq[0][0:1, 0:tb], func=AF.Exp,
                             scale=-0.5))
    p.S(_)
    s_rstd = ss.n

    def _(v):
        v.wait_ge(ss.h, s_rstd)
        sv.inc(v.tensor_tensor(out=bufs["st_b"][:, 0:tb],
                               in0=bufs["st_mean"][:, 0:tb],
                               in1=bufs["st_rstd"][:, 0:tb], op=ALU.mult))
        v.wait_ge(sv.h, sv.n)
        sv.inc(v.tensor_scalar(out=bufs["st_b"][:, 0:tb],
                               in0=bufs["st_b"][:, 0:tb], scalar1=-1.0,
                               scalar2=None, op0=ALU.mult))
    p.V(_)
    v_b = sv.n

    def _(t):
        t.wait_ge(sv.h, v_b)
        wait(t, bf.get("bc"))
        t.matmul(bufs["ps_bc_a"][:, 0:tb], oner[:], bufs["st_rstd"][:, 0:tb],
                 start=True, stop=True)
        st.inc(t.matmul(bufs["ps_bc_b"][:, 0:tb], oner[:], bufs["st_b"][:, 0:tb],
                        start=True, stop=True))
    p.T(_)
    t_bc = st.n

    def _(v):
        v.wait_ge(st.h, t_bc)
        for i in range(d_tiles):
            v.tensor_copy(out=sq[i % 2][:, 0:tb], in_=y_tiles[i])
            v.tensor_tensor(out=sq[i % 2][:, 0:tb], in0=sq[i % 2][:, 0:tb],
                            in1=bufs["ps_bc_a"][:, 0:tb], op=ALU.mult)
            sv.inc(v.tensor_tensor(out=write_out(i), in0=sq[i % 2][:, 0:tb],
                                   in1=bufs["ps_bc_b"][:, 0:tb], op=ALU.add))
    p.V(_)
    bf["bc"] = sv.mark()
    return sv.n



def build_launch1(s_half, upto="all"):
    p = Prog()
    TKV = 1024 * (s_half + 1)
    QG0 = 1024 * s_half
    NTB = TKV // 512

    xTf = p.dram_in("xTfull", (C, T))
    wq = p.dram_in("wq", (128, NCT * C))
    wk = p.dram_in("wk", (128, NCT * 512))
    wv = p.dram_in("wv", (128, NCT * 512))
    wo = p.dram_in("wo", (128, NCT * C))
    gatew = p.dram_in("gatew", (128, NCT * I0))
    upw = p.dram_in("upw", (128, NCT * I0))
    downw = p.dram_in("downw", (128, NI0 * 128 * NCT))
    gsc = p.dram_in("gsc", (128, NI0))
    usc = p.dram_in("usc", (128, NI0))
    dsc = p.dram_in("dsc", (128, NCT))
    win_z = p.dram_in("win_z", (128, NCT * DIN))
    win_x = p.dram_in("win_x", (128, NCT * XBC))
    win_dt = p.dram_in("win_dt", (128, NCT * 128))
    ropec = p.dram_in("ropec", (128, T))
    ropes = p.dram_in("ropes", (128, T))
    masks = p.dram_in("masks", (128, 896))
    ident_i = p.dram_in("ident_i", (128, 128))

    x1T_o = p.dram_out("x1T_o", (C, TL), F32)
    x2T_o = p.dram_out("x2T_o", (C, TL), F32)
    zT_o = p.dram_out("zT_o", (DIN, TL), F32)
    xbcT_o = p.dram_out("xbcT_o", (XBC, TL), F32)
    dt_o = p.dram_out("dt_o", (NH, TL), F32)

    ld = p.sem("ld")
    st = p.sem("st")
    sv = p.sem("sv")
    ss = p.sem("ss")
    so = p.sem("so")

    ones_col = p.sb("ones_col", (128, 1), F32)
    ones_colb = p.sb("ones_colb", (128, 1), BF16)
    ones_row = p.sb("ones_row", (1, 128), F32)
    identb = p.sb("identb", (128, 128), BF16)
    cosT = p.sb("cosT", (128, T), BF16)
    sinT = p.sb("sinT", (128, T), BF16)
    maskt = p.sb("maskt", (128, 896), BF16)
    x1T = p.sb("x1T", (128, NCT * TL), BF16)
    sqscr0 = p.sb("sqscr0", (128, 512), F32)
    sqscr = [sqscr0, sqscr0]
    qh_scr = p.sb("qh_scr", (128, 512), F32)
    rope_scr = p.sb("rope_scr", (128, 512), F32)
    ptile = [p.sb(f"ptile{i}", (128, 512), BF16) for i in range(2)]
    e_scr = p.sb("e_scr", (128, 512), F32)
    o_scr = e_scr
    xres = [p.sb(f"xres{i}", (128, 512), F32) for i in range(2)]
    f_scr = [qh_scr, rope_scr, e_scr]
    scgate = p.sb("scgate", (128, NI0), F32)
    scup = p.sb("scup", (128, NI0), F32)
    scdown = p.sb("scdown", (128, NCT), F32)
    st_mean = p.sb("st_mean", (1, 512), F32)
    st_var = p.sb("st_var", (1, 512), F32)
    st_rstd = st_var
    st_b = p.sb("st_b", (1, 512), F32)
    den_r = st_b

    U96 = p.sb("U96", (128, 49152), BF16)
    U32 = p.sb("U32", (128, 16384), BF16)
    xblk = [U96[:, i * 8192:(i + 1) * 8192] for i in range(2)]
    QT = U96[:, 16384:32768]
    AO = U96[:, 32768:49152]
    fuse = U96[:, 0:24576]
    stage = [U96[:, 32768 + i * 1024: 32768 + (i + 1) * 1024] for i in range(2)]
    KT = U32[:, 0:KV * TKV]
    VT = U32[:, 8192:8192 + (TKV // 128) * 512]
    x2T = U32[:, 0:16384]
    vt_tmp = ptile[0]

    ws = WStream(p, "ws", ld, nci=NCT)
    ws8 = WStream(p, "ws8", ld, nci=24, dtype=F8)

    ps_r = [p.ps(f"ps_r{i}", (128, 512)) for i in range(2)]
    ps_o = p.ps("ps_o", (128, 512))
    pst = p.ps("pst", (128, 128), BF16)
    ps_sum = p.ps("ps_sum", (1, 512))
    ps_den = ps_sum
    ps_sumsq = p.ps("ps_sumsq", (1, 512))
    ps_bc_a = p.ps("ps_bc_a", (128, 512))
    ps_bc_b = p.ps("ps_bc_b", (128, 512))

    sems = (st, sv, ss)
    ln_bufs = dict(ps_sum=ps_sum, ps_sumsq=ps_sumsq, ps_bc_a=ps_bc_a,
                   ps_bc_b=ps_bc_b, sq_scratch=sqscr, st_mean=st_mean,
                   st_var=st_var, st_rstd=st_rstd, st_b=st_b,
                   ones_col=ones_col, ones_row=ones_row,
                   ones_colb=ones_colb)

    ps_free = {0: None, 1: None}
    pctr = [0]

    p.begin()

    def _(v):
        v.memset(ones_col[:], 1.0)
        v.memset(ones_colb[:], 1.0)
        v.memset(U96[:], 0.0)
        v.memset(U32[:], 0.0)
        v.memset(x1T[:], 0.0)
        v.memset(ptile[0][:], 0.0)
        v.memset(ptile[1][:], 0.0)
        v.memset(qh_scr[:], 0.0)
        v.memset(rope_scr[:], 0.0)
        v.memset(e_scr[:], 0.0)
        v.memset(sqscr0[:], 0.0)
        v.memset(st_mean[:], 0.0)
        v.memset(st_var[:], 0.0)
        v.memset(st_b[:], 0.0)
        v.memset(xres[0][:], 0.0)
        v.memset(xres[1][:], 0.0)
        sv.inc(v.memset(ones_row[:], 1.0))
    p.V(_)
    init_done = sv.mark()

    def _(g):
        ld.inc(g.dma_start(out=cosT[:], in_=ropec[:]), 16)
        ld.inc(g.dma_start(out=sinT[:], in_=ropes[:]), 16)
        ld.inc(g.dma_start(out=maskt[:], in_=masks[:]), 16)
        ld.inc(g.dma_start(out=identb[:], in_=ident_i[:]), 16)
        ld.inc(g.dma_start(out=scgate[:], in_=gsc[:]), 16)
        ld.inc(g.dma_start(out=scup[:], in_=usc[:]), 16)
        ld.inc(g.dma_start(out=scdown[:], in_=dsc[:]), 16)
    p.G(_)
    ld_const = ld.n

    def rope_evac(v, psum_ap, cos_off, out_ap):
        v.tensor_copy(out=qh_scr[:], in_=psum_ap)
        v.tensor_scalar(out=rope_scr[0:64, :], in0=qh_scr[64:128, :],
                        scalar1=-1.0, scalar2=None, op0=ALU.mult)
        v.tensor_copy(out=rope_scr[64:128, :], in_=qh_scr[0:64, :])
        v.tensor_tensor(out=rope_scr[:], in0=rope_scr[:],
                        in1=sinT[:, cos_off:cos_off + 512], op=ALU.mult)
        v.tensor_tensor(out=qh_scr[:], in0=qh_scr[:],
                        in1=cosT[:, cos_off:cos_off + 512], op=ALU.mult)
        return v.tensor_tensor(out=out_ap, in0=qh_scr[:], in1=rope_scr[:],
                               op=ALU.add)

    xblk_used = {0: None, 1: None}
    trans_free = [None]
    for tb in range(NTB):
        bi = tb % 2

        def _(g, tb=tb, bi=bi):
            if xblk_used[bi] is None:
                wait(g, init_done)
            wait(g, xblk_used[bi])
            for ci in range(NCT):
                k = 16 if ci == NCT - 1 else 0
                ld.inc(g.dma_start(
                    out=xblk[bi][:, ci * 512:(ci + 1) * 512],
                    in_=xTf[ci * 128:(ci + 1) * 128, tb * 512:(tb + 1) * 512]),
                    16)
        p.G(_)
        ld_xb = ld.n
        last_mark = None

        for g_kv in range(KV):
            buf, seq = ws.load(wk, g_kv, NCT)
            pi = pctr[0] % 2
            pctr[0] += 1

            def _(t, g_kv=g_kv, tb=tb, bi=bi, pi=pi, seq=seq, buf=buf,
                  ld_xb=ld_xb):
                t.wait_ge(ld.h, ws.ld_val[seq])
                t.wait_ge(ld.h, ld_xb)
                wait(t, init_done)
                wait(t, ps_free[pi])
                for ci in range(NCT):
                    ins = t.matmul(ps_r[pi][:],
                                   buf[:, ci * 128:(ci + 1) * 128],
                                   xblk[bi][:, ci * 512:(ci + 1) * 512],
                                   start=(ci == 0), stop=(ci == NCT - 1))
                st.inc(ins)
            p.T(_)
            ws.mark_done(st.mark())
            mm = st.mark()

            def _(v, g_kv=g_kv, tb=tb, pi=pi, mm=mm):
                wait(v, mm)
                sv.inc(rope_evac(v, ps_r[pi][:], tb * 512,
                                 KT[:, g_kv * TKV + tb * 512:
                                     g_kv * TKV + tb * 512 + 512]))
            p.V(_)
            ps_free[pi] = sv.mark()
            last_mark = sv.mark()

        for j in range(4):
            buf, seq = ws.load(wv, j, NCT)
            pi = pctr[0] % 2
            pctr[0] += 1

            def _(t, j=j, tb=tb, bi=bi, pi=pi, seq=seq, buf=buf, ld_xb=ld_xb):
                t.wait_ge(ld.h, ws.ld_val[seq])
                t.wait_ge(ld.h, ld_xb)
                wait(t, ps_free[pi])
                for ci in range(NCT):
                    ins = t.matmul(ps_r[pi][:],
                                   buf[:, ci * 128:(ci + 1) * 128],
                                   xblk[bi][:, ci * 512:(ci + 1) * 512],
                                   start=(ci == 0), stop=(ci == NCT - 1))
                st.inc(ins)
            p.T(_)
            ws.mark_done(st.mark())
            mm = st.mark()

            def _(v, pi=pi, mm=mm):
                wait(v, mm)
                sv.inc(v.tensor_copy(out=vt_tmp[:], in_=ps_r[pi][:]))
            p.V(_)
            ps_free[pi] = sv.mark()
            ev = sv.mark()

            for k in range(4):
                def _(t, k=k, ev=ev):
                    wait(t, ev)
                    wait(t, trans_free[0])
                    t.wait_ge(ld.h, ld_const)
                    st.inc(t.transpose(pst[:], vt_tmp[:, k * 128:(k + 1) * 128],
                                       identb[:]))
                p.T(_)
                tmark = st.mark()

                def _(v, j=j, tb=tb, k=k, tmark=tmark):
                    wait(v, tmark)
                    tt = tb * 4 + k
                    sv.inc(v.tensor_copy(
                        out=VT[:, tt * 512 + j * 128: tt * 512 + (j + 1) * 128],
                        in_=pst[:]))
                p.V(_)
                trans_free[0] = sv.mark()
            last_mark = sv.mark()

        if QG0 <= tb * 512 < QG0 + 1024:
            qb = (tb * 512 - QG0) // 512
            for h in range(H):
                buf, seq = ws.load(wq, h, NCT)
                pi = pctr[0] % 2
                pctr[0] += 1

                def _(t, h=h, tb=tb, bi=bi, pi=pi, seq=seq, buf=buf,
                      ld_xb=ld_xb):
                    t.wait_ge(ld.h, ws.ld_val[seq])
                    t.wait_ge(ld.h, ld_xb)
                    wait(t, ps_free[pi])
                    for ci in range(NCT):
                        ins = t.matmul(ps_r[pi][:],
                                       buf[:, ci * 128:(ci + 1) * 128],
                                       xblk[bi][:, ci * 512:(ci + 1) * 512],
                                       start=(ci == 0), stop=(ci == NCT - 1))
                    st.inc(ins)
                p.T(_)
                ws.mark_done(st.mark())
                mm = st.mark()

                def _(v, h=h, qb=qb, tb=tb, pi=pi, mm=mm):
                    wait(v, mm)
                    sv.inc(rope_evac(v, ps_r[pi][:], tb * 512,
                                     QT[:, h * TL + qb * 512:
                                         h * TL + qb * 512 + 512]))
                p.V(_)
                ps_free[pi] = sv.mark()
            last_mark = sv.mark()
        xblk_used[bi] = last_mark
    qkv_done = sv.mark()

    if upto == "qkv":
        qt_o = p.dram_out("qt_o", (H * 128, TL), F32)
        kt_o = p.dram_out("kt_o", (KV * 128, TKV), F32)
        vt_o = p.dram_out("vt_o", ((TKV // 128) * 128, 512), F32)

        def _(yq):
            yq.wait_ge(sv.h, qkv_done[1])
            for hh in range(H):
                so.inc(yq.dma_start(out=qt_o[hh * 128:(hh + 1) * 128, :],
                                    in_=QT[:, hh * TL:(hh + 1) * TL]), 16)
            for gg in range(KV):
                so.inc(yq.dma_start(out=kt_o[gg * 128:(gg + 1) * 128, :],
                                    in_=KT[:, gg * TKV:(gg + 1) * TKV]), 16)
            for tt2 in range(TKV // 128):
                so.inc(yq.dma_start(out=vt_o[tt2 * 128:(tt2 + 1) * 128, :],
                                    in_=VT[:, tt2 * 512:(tt2 + 1) * 512]), 16)
            yq.wait_ge(so.h, so.n)
        p.G(_)
        return p

    exp_free = {0: None, 1: None}
    grp_free = None
    for h in range(H):
        g_kv = h // (H // KV)
        for qb in range(2):
            qg0 = QG0 + qb * 512
            nk = (qg0 + 512) // 128
            for ki in range(nk):
                pi = pctr[0] % 2
                pctr[0] += 1

                def _(t, h=h, g_kv=g_kv, qb=qb, ki=ki, pi=pi):
                    wait(t, ps_free[pi])
                    if h == 0 and qb == 0 and ki == 0:
                        wait(t, qkv_done)
                    st.inc(t.matmul(
                        ps_r[pi][:],
                        KT[:, g_kv * TKV + ki * 128: g_kv * TKV + (ki + 1) * 128],
                        QT[:, h * TL + qb * 512: h * TL + qb * 512 + 512],
                        start=True, stop=True))
                p.T(_)
                s_mm = st.mark()

                ei = ki % 2
                m_off = ki - qg0 // 128
                is_diag = 0 <= m_off <= 3

                def _(sc, pi=pi, ei=ei, s_mm=s_mm):
                    wait(sc, s_mm)
                    wait(sc, exp_free[ei])
                    ss.inc(sc.activation(out=ptile[ei][:], in_=ps_r[pi][:],
                                         func=AF.Exp, scale=1.0 / math.sqrt(D)))
                p.S(_)
                e_done = ss.mark()
                ps_free[pi] = e_done

                if is_diag:
                    def _(v, ei=ei, m_off=m_off, e_done=e_done):
                        wait(v, e_done)
                        off = 384 - 128 * m_off
                        sv.inc(v.tensor_tensor(
                            out=ptile[ei][:], in0=ptile[ei][:],
                            in1=maskt[:, off:off + 512], op=ALU.mult))
                    p.V(_)
                    m_done = sv.mark()
                else:
                    m_done = None

                def _(t, g_kv=g_kv, ki=ki, ei=ei, nk=nk, m_done=m_done,
                      e_done=e_done):
                    if m_done is not None:
                        wait(t, m_done)
                    else:
                        wait(t, e_done)
                    if ki == 0:
                        wait(t, grp_free)
                        wait(t, ln_bufs.get("free", {}).get("stats"))
                    t.matmul(ps_den[:], ones_colb[:], ptile[ei][:],
                             start=(ki == 0), stop=(ki == nk - 1))
                    st.inc(t.matmul(
                        ps_o[:],
                        VT[:, ki * 512 + g_kv * 128: ki * 512 + (g_kv + 1) * 128],
                        ptile[ei][:], start=(ki == 0), stop=(ki == nk - 1)))
                p.T(_)
                exp_free[ei] = st.mark()
            pv_done = st.mark()

            def _(v, pv_done=pv_done):
                wait(v, pv_done)
                v.reciprocal(out=den_r[:], in_=ps_den[:])
                sv.inc(v.tensor_copy(out=o_scr[:], in_=ps_o[:]))
            p.V(_)
            eps_done = sv.mark()
            ln_bufs.setdefault("free", {})["stats"] = eps_done

            def _(t, eps_done=eps_done):
                wait(t, eps_done)
                wait(t, ln_bufs.get("free", {}).get("bc"))
                st.inc(t.matmul(ps_bc_a[:], ones_row[:], den_r[:],
                                start=True, stop=True))
            p.T(_)
            bc_done = st.mark()

            def _(v, h=h, qb=qb, bc_done=bc_done):
                wait(v, bc_done)
                sv.inc(v.tensor_tensor(
                    out=AO[:, h * TL + qb * 512: h * TL + qb * 512 + 512],
                    in0=o_scr[:], in1=ps_bc_a[:], op=ALU.mult))
            p.V(_)
            grp_free = sv.mark()
            ln_bufs.setdefault("free", {})["bc"] = grp_free

    if upto == "ao":
        ao_o = p.dram_out("ao_o", (H * 128, TL), F32)

        def _(yq):
            wait(yq, grp_free)
            for hh in range(H):
                so.inc(yq.dma_start(out=ao_o[hh * 128:(hh + 1) * 128, :],
                                    in_=AO[:, hh * TL:(hh + 1) * TL]), 16)
            yq.wait_ge(so.h, so.n)
        p.G(_)
        return p

    attn_all = grp_free
    for tb in range(2):
        xres_b = xblk[(QG0 // 512 + tb) % 2]
        for i in range(NCT):
            buf, seq = ws.load(wo, i, NCT)
            pi = pctr[0] % 2
            pctr[0] += 1

            def _(t, i=i, tb=tb, pi=pi, seq=seq, buf=buf, attn_all=attn_all):
                t.wait_ge(ld.h, ws.ld_val[seq])
                wait(t, ps_free[pi])
                if tb == 0 and i == 0:
                    wait(t, attn_all)
                for ci in range(NCT):
                    ins = t.matmul(
                        ps_r[pi][:], buf[:, ci * 128:(ci + 1) * 128],
                        AO[:, ci * TL + tb * 512: ci * TL + tb * 512 + 512],
                        start=(ci == 0), stop=(ci == NCT - 1))
                st.inc(ins)
            p.T(_)
            ws.mark_done(st.mark())
            mm = st.mark()

            def _(v, i=i, tb=tb, pi=pi, mm=mm, xres_b=xres_b):
                wait(v, mm)
                v.tensor_copy(out=e_scr[:], in_=ps_r[pi][:])
                sv.inc(v.tensor_tensor(
                    out=x1T[:, i * TL + tb * 512: i * TL + tb * 512 + 512],
                    in0=e_scr[:], in1=xres_b[:, i * 512:(i + 1) * 512],
                    op=ALU.add))
            p.V(_)
            ps_free[pi] = sv.mark()
        if upto == "y":
            continue
        ln_emit(p, sems,
                [x1T[:, i * TL + tb * 512: i * TL + tb * 512 + 512]
                 for i in range(NCT)],
                (lambda i, tb=tb: x1T[:, i * TL + tb * 512:
                                      i * TL + tb * 512 + 512]), ln_bufs,
                ready=sv.mark())
    x1_done = sv.n

    def _(g):
        g.wait_ge(sv.h, x1_done)
        for i in range(NCT):
            so.inc(g.dma_start(out=x1T_o[i * 128:(i + 1) * 128, :],
                               in_=x1T[:, i * TL:(i + 1) * TL]), 16)
    p.G(_)
    x1_stored = so.n

    if upto == "y":
        ao2_o = p.dram_out("ao2_o", (H * 128, TL), F32)

        def _(g):
            g.wait_ge(sv.h, x1_done)
            for hh in range(H):
                so.inc(g.dma_start(out=ao2_o[hh * 128:(hh + 1) * 128, :],
                                   in_=AO[:, hh * TL:(hh + 1) * TL]), 16)
        p.G(_)
        fin = so.n

        def _(yq):
            yq.wait_ge(so.h, fin)
        p.Y(_)
        return p

    if upto == "x1":
        x1b_o = p.dram_out("x1b_o", (C, TL), BF16)

        def _(yq):
            yq.wait_ge(sv.h, x1_done)
            for i in range(NCT):
                so.inc(yq.dma_start(out=x1b_o[i * 128:(i + 1) * 128, :],
                                    in_=x1T[:, i * TL:(i + 1) * TL]), 16)
            yq.wait_ge(so.h, so.n)
        p.Y(_)
        return p

    silu_prev = None
    fuse_prev = None
    m_x1 = (sv, x1_done)
    down_done = None
    for tb in range(2):
        for fi in range(NI0):
            bufg, seqg = ws8.load(gatew, fi, NCT)
            pig = pctr[0] % 2
            pctr[0] += 1

            def _(t, fi=fi, tb=tb, pig=pig, seqg=seqg, bufg=bufg):
                t.wait_ge(ld.h, ws8.ld_val[seqg])
                wait(t, ps_free[pig])
                if fi == 0 and tb == 0:
                    wait(t, m_x1)
                for ci in range(NCT):
                    ins = t.matmul(
                        ps_r[pig][:], bufg[:, ci * 128:(ci + 1) * 128],
                        x1T[:, ci * TL + tb * 512: ci * TL + tb * 512 + 512],
                        start=(ci == 0), stop=(ci == NCT - 1))
                st.inc(ins)
            p.T(_)
            ws8.mark_done(st.mark())
            mmg = st.mark()

            def _(v, fi=fi, pig=pig, mmg=mmg):
                wait(v, mmg)
                wait(v, silu_prev)
                sv.inc(v.tensor_scalar(out=f_scr[0][:], in0=ps_r[pig][:],
                                       scalar1=scgate[:, fi:fi + 1],
                                       scalar2=None, op0=ALU.mult))
            p.V(_)
            ps_free[pig] = sv.mark()
            g_evac = sv.mark()

            def _(sc, g_evac=g_evac):
                wait(sc, g_evac)
                wait(sc, fuse_prev)
                ss.inc(sc.activation(out=f_scr[2][:], in_=f_scr[0][:],
                                     func=AF.Silu))
            p.S(_)
            silu_prev = ss.mark()

            bufu, sequ = ws8.load(upw, fi, NCT)
            piu = pctr[0] % 2
            pctr[0] += 1

            def _(t, fi=fi, tb=tb, piu=piu, sequ=sequ, bufu=bufu):
                t.wait_ge(ld.h, ws8.ld_val[sequ])
                wait(t, ps_free[piu])
                for ci in range(NCT):
                    ins = t.matmul(
                        ps_r[piu][:], bufu[:, ci * 128:(ci + 1) * 128],
                        x1T[:, ci * TL + tb * 512: ci * TL + tb * 512 + 512],
                        start=(ci == 0), stop=(ci == NCT - 1))
                st.inc(ins)
            p.T(_)
            ws8.mark_done(st.mark())
            mmu = st.mark()

            def _(v, fi=fi, tb=tb, piu=piu, mmu=mmu):
                wait(v, mmu)
                wait(v, silu_prev)
                if fi == 0 and tb == 1:
                    wait(v, down_done)
                v.tensor_scalar(out=f_scr[1][:], in0=ps_r[piu][:],
                                scalar1=scup[:, fi:fi + 1], scalar2=None,
                                op0=ALU.mult)
                sv.inc(v.tensor_tensor(out=fuse[:, fi * 512:(fi + 1) * 512],
                                       in0=f_scr[2][:], in1=f_scr[1][:],
                                       op=ALU.mult))
            p.V(_)
            ps_free[piu] = sv.mark()
            fuse_prev = sv.mark()
        fuse_done = sv.mark()

        for i in range(NCT):
            pid = pctr[0] % 2
            pctr[0] += 1
            b0, s0_ = ws8.load(downw, 2 * i, 24)
            b1, s1_ = ws8.load(downw, 2 * i + 1, 24)

            def _(t, i=i, pid=pid, b0=b0, s0_=s0_, b1=b1, s1_=s1_,
                  fuse_done=fuse_done):
                t.wait_ge(ld.h, ws8.ld_val[s1_])
                wait(t, ps_free[pid])
                if i == 0:
                    wait(t, fuse_done)
                for ci in range(48):
                    buf = b0 if ci < 24 else b1
                    cc = ci % 24
                    ins = t.matmul(ps_r[pid][:],
                                   buf[:, cc * 128:(cc + 1) * 128],
                                   fuse[:, ci * 512:(ci + 1) * 512],
                                   start=(ci == 0), stop=(ci == 47))
                st.inc(ins)
            p.T(_)
            ws8.mark_done(st.mark())
            ws8.done[ws8.seq - 2] = st.mark()
            mmd = st.mark()

            def _(v, i=i, tb=tb, pid=pid, mmd=mmd):
                wait(v, mmd)
                v.tensor_scalar(out=f_scr[0][:], in0=ps_r[pid][:],
                                scalar1=scdown[:, i:i + 1], scalar2=None,
                                op0=ALU.mult)
                sv.inc(v.tensor_tensor(
                    out=x2T[:, i * TL + tb * 512: i * TL + tb * 512 + 512],
                    in0=f_scr[0][:],
                    in1=x1T[:, i * TL + tb * 512: i * TL + tb * 512 + 512],
                    op=ALU.add))
            p.V(_)
            ps_free[pid] = sv.mark()
        down_done = st.mark()
        ln_emit(p, sems,
                [x2T[:, i * TL + tb * 512: i * TL + tb * 512 + 512]
                 for i in range(NCT)],
                (lambda i, tb=tb: x2T[:, i * TL + tb * 512:
                                      i * TL + tb * 512 + 512]), ln_bufs,
                ready=sv.mark())
    x2_done = sv.n

    def _(g):
        g.wait_ge(sv.h, x2_done)
        for i in range(NCT):
            so.inc(g.dma_start(out=x2T_o[i * 128:(i + 1) * 128, :],
                               in_=x2T[:, i * TL:(i + 1) * TL]), 16)
    p.G(_)
    x2_stored = so.n

    if upto == "x2":
        def _(yq):
            yq.wait_ge(so.h, x2_stored)
        p.Y(_)
        return p

    stage_free = {0: None, 1: None}
    sctr = [0]
    first_flag = [True]

    def inproj_tile(dram, fi, out_dram, out_row, nrows=128, to_dt=False):
        buf, seq = ws.load(dram, fi, NCT)
        sj = sctr[0] % 2
        sctr[0] += 1
        evacs = None
        for tbb in range(2):
            pi = pctr[0] % 2
            pctr[0] += 1

            def _(t, tbb=tbb, pi=pi, seq=seq, buf=buf, nrows=nrows):
                t.wait_ge(ld.h, ws.ld_val[seq])
                wait(t, ps_free[pi])
                if first_flag[0]:
                    t.wait_ge(sv.h, x2_done)
                    first_flag[0] = False
                for ci in range(NCT):
                    ins = t.matmul(
                        ps_r[pi][0:nrows, 0:512],
                        buf[:, ci * 128: ci * 128 + nrows],
                        x2T[:, ci * TL + tbb * 512: ci * TL + tbb * 512 + 512],
                        start=(ci == 0), stop=(ci == NCT - 1))
                st.inc(ins)
            p.T(_)
            mm = st.mark()

            def _(v, tbb=tbb, pi=pi, sj=sj, mm=mm, nrows=nrows, to_dt=to_dt):
                wait(v, mm)
                if tbb == 0 and not to_dt:
                    wait(v, stage_free[sj])
                if tbb == 0 and to_dt:
                    wait(v, stage_free[sj])
                sv.inc(v.tensor_copy(
                    out=stage[sj][0:nrows, tbb * 512: tbb * 512 + 512],
                    in_=ps_r[pi][0:nrows, 0:512]))
            p.V(_)
            ps_free[pi] = sv.mark()
            evacs = sv.mark()
        ws.mark_done(st.mark())

        def _(g, sj=sj, evacs=evacs, out_row=out_row, nrows=nrows,
              out_dram=out_dram):
            wait(g, evacs)
            so.inc(g.dma_start(out=out_dram[out_row:out_row + nrows, :],
                               in_=stage[sj][0:nrows, :]), 16)
        p.G(_)
        stage_free[sj] = so.mark()
        return evacs

    for fi in range(NDIN):
        inproj_tile(win_z, fi, zT_o, fi * 128)
    for fi in range(NXBC):
        inproj_tile(win_x, fi, xbcT_o, fi * 128)
    inproj_tile(win_dt, 0, dt_o, 0, nrows=64, to_dt=True)
    all_stored = so.n

    def _(yq):
        yq.wait_ge(so.h, all_stored)
    p.Y(_)
    return p



def prep_launch1_inputs(inp):
    x = np.asarray(inp["L0_attn_x"])
    cos_t, sin_t = _rope_tables()
    masks = _attn_masks()
    W_in = np.asarray(inp["L1_attn_W_in"])
    shared = dict(
        wq=_slabc(np.asarray(inp["L0_attn_Wq"])),
        wk=_slabc(np.asarray(inp["L0_attn_Wk"])),
        wv=_slabc(np.asarray(inp["L0_attn_Wv"])),
        wo=_slabc(np.asarray(inp["L0_attn_Wo"])),
        gatew=_slabc(np.asarray(inp["L0_ffn_gate_w"])),
        upw=_slabc(np.asarray(inp["L0_ffn_up_w"])),
        downw=_slabc(np.asarray(inp["L0_ffn_down_w"])),
        gsc=_cols(np.asarray(inp["L0_ffn_gate_scale"])),
        usc=_cols(np.asarray(inp["L0_ffn_up_scale"])),
        dsc=_cols(np.asarray(inp["L0_ffn_down_scale"])),
        win_z=_slabc(W_in[:, 0:DIN]),
        win_x=_slabc(W_in[:, DIN:DIN + XBC]),
        win_dt=_slabc(np.concatenate(
            [W_in[:, DIN + XBC:], np.zeros((C, 64), W_in.dtype)], axis=1)),
        ropec=cos_t, ropes=sin_t, masks=masks,
        ident_i=np.eye(128, dtype=np.float32),
    )
    maps = []
    for core in range(8):
        b = core // 2
        m = dict(shared)
        m["xTfull"] = np.ascontiguousarray(x[b].T)
        maps.append(m)
    return maps





NCHUNK = 16
STR = 2052


def build_launch2(debug=False):
    p = Prog()
    NT = 18

    xbc_in = p.dram_in("xbc_in", (NT * 128, T))
    zh_in = p.dram_in("zh_in", (T, 2048))
    dt_in = p.dram_in("dt_in", (32, T))
    convw_in = p.dram_in("convw_in", (128, NT * 4))
    convb_in = p.dram_in("convb_in", (128, NT))
    alog_in = p.dram_in("alog_in", (32, 1))
    dtb_in = p.dram_in("dtb_in", (32, 1))
    d_in = p.dram_in("d_in", (1, 32))
    triu_in = p.dram_in("triu_in", (128, 128))
    maskji_in = p.dram_in("maskji_in", (128, 128))
    ident_in = p.dram_in("ident_in2", (128, 128))
    onehot_in = p.dram_in("onehot_in", (128, 1))

    t_tok_o = p.dram_out("t_tok_o", (T, 2048), F32)
    psq_o = p.dram_out("psq_o", (128, NCHUNK), F32)

    xbca_d = p.nc.dram_tensor("xbca_d", [NT * 128, T], BF16,
                              kind="Internal").ap()

    ld = p.sem("ld")
    so = p.sem("so")
    st = p.sem("st")
    sv = p.sem("sv")
    ss = p.sem("ss")

    ones_row = p.sb("ones_row", (1, 128), F32)
    identb = p.sb("identb", (128, 128), BF16)
    identf = p.sb("identf", (128, 128), F32)
    triu = p.sb("triu", (128, 128), F32)
    maskji = p.sb("maskji", (128, 128), F32)
    onehot = p.sb("onehot", (128, 1), F32)
    convw = p.sb("convw", (128, NT * 4), F32)
    convb = p.sb("convb", (128, NT), F32)
    alog = p.sb("alog", (32, 1), F32)
    dtb = p.sb("dtb", (32, 1), F32)
    d_row = p.sb("d_row", (1, 32), F32)
    expa = p.sb("expa", (32, 1), F32)
    xbc_pad = p.sb("xbc_pad", (128, NT * STR), BF16)
    conv_in = p.sb("conv_in", (128, T + 4), F32)
    conv_acc = p.sb("conv_acc", (128, T), F32)
    conv_sig = p.sb("conv_sig", (128, T), F32)
    dtf = p.sb("dtf", (128, T), F32)
    dtA = p.sb("dtA", (128, T), F32)
    xbca_c = [p.sb(f"xbca_c{i}", (128, NT * 128), BF16) for i in range(2)]
    X_tok = p.sb("X_tok", (128, 16 * 128), BF16)
    B_tok = p.sb("B_tok", (128, 128), F32)
    prev = [p.sb(f"prev{i}", (128, 32 * 64), F32) for i in range(2)]
    CBm = p.sb("CBm", (128, 128), F32)
    m_scr = p.sb("m_scr", (128, 128), BF16)
    wB = p.sb("wB", (128, 128), BF16)
    prevS = p.sb("prevS", (128, 64), BF16)
    y_scr = p.sb("y_scr", (128, 64), F32)
    x_scr = p.sb("x_scr", (128, 64), F32)
    c_scr = p.sb("c_scr", (128, 64), F32)
    y_chunk = p.sb("y_chunk", (128, 32 * 64), F32)
    cum_T = p.sb("cum_T", (128, 32), F32)
    cumS_T = p.sb("cumS_T", (128, 32), F32)
    dtAT = p.sb("dtAT", (128, 128), F32)
    dtfT = p.sb("dtfT", (128, 128), F32)
    En = p.sb("En", (128, 32), F32)
    Ep = p.sb("Ep", (128, 32), F32)
    w2T = p.sb("w2T", (128, 32), F32)
    Atot_bc = p.sb("Atot_bc", (128, 32), F32)
    esl_bc = p.sb("esl_bc", (128, 32), F32)
    eS_bc = p.sb("eS_bc", (128, 32), F32)
    D_bc = p.sb("D_bc", (128, 32), F32)
    r_mid = p.sb("r_mid", (1, 32), F32)
    r_127 = p.sb("r_127", (1, 32), F32)
    r_tmp = p.sb("r_tmp", (1, 32), F32)
    r_esl = p.sb("r_esl", (1, 32), F32)
    mid_bc = p.sb("mid_bc", (128, 32), F32)
    if debug:
        snap1 = p.sb("snap1", (128, 32), F32)
        snap2 = p.sb("snap2", (128, 32), F32)
        snap3 = p.sb("snap3", (128, 32), F32)
    zf = [p.sb(f"zf{i}", (128, 2048), F32) for i in range(2)]
    zsig = conv_sig
    z2 = conv_acc
    t_stage1 = p.sb("t_stage1", (128, 2048), F32)
    t_stage = [conv_in[:, 0:2048], t_stage1[:, 0:2048]]
    psq_sb = p.sb("psq_sb", (128, NCHUNK), F32)
    stg = [p.sb("stg0", (128, T), BF16)] * 2

    pst_b = p.ps("pst_b", (128, 128), BF16)
    ps_f = p.ps("ps_f", (128, 128))
    ps_cb = p.ps("ps_cb", (128, 128))
    ps_y = [p.ps(f"ps_y{i}", (128, 64)) for i in range(2)]
    ps_st = [p.ps(f"ps_st{i}", (128, 64)) for i in range(2)]

    p.begin()

    def _(v):
        v.memset(ones_row[:], 1.0)
        v.memset(xbc_pad[:], 0.0)
        v.memset(dtf[:], 0.0)
        v.memset(dtA[:], 0.0)
        v.memset(prev[0][:], 0.0)
        v.memset(prev[1][:], 0.0)
        v.memset(conv_in[:], 0.0)
        sv.inc(v.memset(psq_sb[:], 0.0))
    p.V(_)
    init_done = sv.mark()

    def _(g):
        wait(g, init_done)
        for name, dst, src in [
            ("identb", identb, ident_in), ("identf", identf, ident_in),
            ("triu", triu, triu_in), ("maskji", maskji, maskji_in),
            ("onehot", onehot, onehot_in), ("convw", convw, convw_in),
            ("convb", convb, convb_in), ("alog", alog, alog_in),
            ("dtb", dtb, dtb_in), ("d_row", d_row, d_in),
        ]:
            ld.inc(g.dma_start(out=dst[:], in_=src[:]), 16)
        for ti in range(NT):
            ld.inc(g.dma_start(
                out=xbc_pad[:, ti * STR + 3: ti * STR + 3 + T],
                in_=xbc_in[ti * 128:(ti + 1) * 128, :]), 16)
        ld.inc(g.dma_start(out=dtf[0:32, :], in_=dt_in[:]), 16)
    p.G(_)
    ld_init = ld.n

    def _(v):
        v.wait_ge(ld.h, ld_init)
        sv.inc(v.tensor_scalar(out=dtA[0:32, :], in0=dtf[0:32, :],
                               scalar1=dtb[:, 0:1], scalar2=None, op0=ALU.add))
    p.V(_)
    v_dtb = sv.n

    def _(sc):
        sc.wait_ge(sv.h, v_dtb)
        sc.activation(out=conv_acc[0:32, :], in_=dtA[0:32, :], func=AF.Abs)
        sc.activation(out=conv_sig[0:32, :], in_=conv_acc[0:32, :],
                      func=AF.Exp, scale=-1.0)
        sc.activation(out=conv_acc[0:32, :], in_=conv_sig[0:32, :],
                      func=AF.Ln, bias=1.0)
        sc.activation(out=conv_sig[0:32, :], in_=dtA[0:32, :], func=AF.Relu)
        ss.inc(sc.activation(out=expa[:], in_=alog[:], func=AF.Exp))
    p.S(_)
    s_sp = ss.n

    def _(v):
        v.wait_ge(ss.h, s_sp)
        v.tensor_tensor(out=dtf[0:32, :], in0=conv_acc[0:32, :],
                        in1=conv_sig[0:32, :], op=ALU.add)
        sv.inc(v.tensor_scalar(out=dtA[0:32, :], in0=dtf[0:32, :],
                               scalar1=expa[:, 0:1], scalar2=-1.0,
                               op0=ALU.mult, op1=ALU.mult))
    p.V(_)
    dt_done = sv.mark()

    def _(t):
        wait(t, dt_done)
        st.inc(t.matmul(ps_f[:, 0:32], ones_row[:], d_row[:], start=True,
                        stop=True))
    p.T(_)
    t_dbc = st.mark()

    def _(v):
        wait(v, t_dbc)
        sv.inc(v.tensor_copy(out=D_bc[:], in_=ps_f[:, 0:32]))
    p.V(_)
    dbc_done = sv.mark()

    conv_store = {0: None, 1: None}
    for ti in range(NT):
        sj = ti % 2

        def _(v, ti=ti, sj=sj):
            if ti == 0:
                v.wait_ge(ld.h, ld_init)
            wait(v, conv_store[0])
            wait(v, conv_store[1])
            base = ti * STR
            v.tensor_copy(out=conv_in[:, 0:T + 3],
                          in_=xbc_pad[:, base: base + T + 3])
            v.tensor_scalar(out=conv_acc[:], in0=conv_in[:, 0:T],
                            scalar1=convw[:, ti * 4: ti * 4 + 1],
                            scalar2=None, op0=ALU.mult)
            for k2 in range(1, 4):
                v.scalar_tensor_tensor(out=conv_acc[:],
                                       in0=conv_in[:, k2: k2 + T],
                                       scalar=convw[:, ti * 4 + k2: ti * 4 + k2 + 1],
                                       in1=conv_acc[:], op0=ALU.mult,
                                       op1=ALU.add)
            sv.inc(v.tensor_scalar(out=conv_acc[:], in0=conv_acc[:],
                                   scalar1=convb[:, ti: ti + 1], scalar2=None,
                                   op0=ALU.add))
        p.V(_)
        v_acc = sv.n

        def _(sc, v_acc=v_acc):
            sc.wait_ge(sv.h, v_acc)
            ss.inc(sc.activation(out=conv_sig[:], in_=conv_acc[:],
                                 func=AF.Sigmoid))
        p.S(_)
        s_sig = ss.n

        def _(v, sj=sj, s_sig=s_sig):
            v.wait_ge(ss.h, s_sig)
            sv.inc(v.tensor_tensor(out=stg[sj][:], in0=conv_acc[:],
                                   in1=conv_sig[:], op=ALU.mult))
        p.V(_)
        ev = sv.mark()

        def _(g, ti=ti, sj=sj, ev=ev):
            wait(g, ev)
            so.inc(g.dma_start(out=xbca_d[ti * 128:(ti + 1) * 128, :],
                               in_=stg[sj][:]), 16)
        p.G(_)
        conv_store[sj] = so.mark()
    conv_all = so.n

    xbca_ld = {0: None, 1: None}
    chunk_done = {0: None, 1: None}
    zf_ld = {0: None, 1: None}
    zf_used = {0: None, 1: None}
    tst_used = {0: None, 1: None}
    pst_free = None
    psf_free = None
    pscb_free = None
    psy_free = {0: None, 1: None}
    psst_free = {0: None, 1: None}
    tst_st = {0: None, 1: None}

    for c in range(NCHUNK):
        cj = c % 2
        def _(yq, c=c, cj=cj):
            if c < 2:
                yq.wait_ge(so.h, conv_all)
            wait(yq, chunk_done[cj])
            for ti in range(NT):
                so.inc(yq.dma_start(
                    out=xbca_c[cj][:, ti * 128:(ti + 1) * 128],
                    in_=xbca_d[ti * 128:(ti + 1) * 128, c * 128:(c + 1) * 128]),
                    16)
        p.Y(_)
        xbca_ld[cj] = so.mark()

        def _(yq, c=c, cj=cj):
            wait(yq, zf_used[cj])
            if c < 2:
                wait(yq, (sv, 1))
            so.inc(yq.dma_start(out=zf[cj][:],
                                in_=zh_in[c * 128:(c + 1) * 128, :]), 16)
        p.Y(_)
        zf_ld[cj] = so.mark()

        for ti in range(17):
            def _(t, c=c, cj=cj, ti=ti):
                wait(t, xbca_ld[cj])
                wait(t, pst_free)
                if c == 0 and ti == 0:
                    wait(t, dbc_done)
                st.inc(t.transpose(pst_b[:],
                                   xbca_c[cj][:, ti * 128:(ti + 1) * 128],
                                   identb[:]))
            p.T(_)
            tmk = st.mark()

            def _(v, ti=ti, tmk=tmk):
                wait(v, tmk)
                if ti < 16:
                    sv.inc(v.tensor_copy(out=X_tok[:, ti * 128:(ti + 1) * 128],
                                         in_=pst_b[:]))
                else:
                    sv.inc(v.tensor_copy(out=B_tok[:], in_=pst_b[:]))
            p.V(_)
            pst_free = sv.mark()
        x_tok_done = sv.mark()

        def _(t, c=c):
            wait(t, dt_done)
            wait(t, psf_free)
            st.inc(t.transpose(ps_f[:], dtA[:, c * 128:(c + 1) * 128],
                               identf[:]))
        p.T(_)
        tmk = st.mark()

        def _(v, tmk=tmk):
            wait(v, tmk)
            sv.inc(v.tensor_copy(out=dtAT[:], in_=ps_f[:]))
        p.V(_)
        psf_free = sv.mark()

        def _(t, c=c):
            wait(t, psf_free)
            st.inc(t.transpose(ps_f[:], dtf[:, c * 128:(c + 1) * 128],
                               identf[:]))
        p.T(_)
        tmk = st.mark()

        def _(v, tmk=tmk):
            wait(v, tmk)
            sv.inc(v.tensor_copy(out=dtfT[:], in_=ps_f[:]))
        p.V(_)
        psf_free = sv.mark()
        dt_T_done = sv.mark()

        def _(t, c=c):
            wait(t, dt_T_done)
            st.inc(t.matmul(ps_f[:, 0:32], triu[:], dtAT[:, 0:32],
                            start=True, stop=True))
        p.T(_)
        tmk = st.mark()

        def _(v, tmk=tmk):
            wait(v, tmk)
            sv.inc(v.tensor_copy(out=cum_T[:], in_=ps_f[:, 0:32]))
        p.V(_)
        cum_done = sv.mark()
        psf_free = cum_done

        def _(t):
            wait(t, cum_done)
            st.inc(t.matmul(ps_f[0:1, 0:32], onehot[:], cum_T[:], start=True,
                            stop=True))
        p.T(_)
        tmk = st.mark()

        def _(v, tmk=tmk):
            wait(v, tmk)
            sv.inc(v.tensor_copy(out=r_127[:], in_=ps_f[0:1, 0:32]))
            v.wait_ge(sv.h, sv.n)
            sv.inc(v.tensor_tensor(out=r_tmp[:], in0=cum_T[0:1, 0:32],
                                   in1=r_127[:], op=ALU.add))
            v.wait_ge(sv.h, sv.n)
            sv.inc(v.tensor_scalar(out=r_mid[:], in0=r_tmp[:], scalar1=0.5,
                                   scalar2=None, op0=ALU.mult))
        p.V(_)
        mid_done = sv.mark()
        psf_free = mid_done

        def _(t):
            wait(t, mid_done)
            st.inc(t.matmul(ps_f[:, 0:32], ones_row[:], r_mid[:], start=True,
                            stop=True))
        p.T(_)
        tmk = st.mark()

        def _(v, tmk=tmk):
            wait(v, tmk)
            sv.inc(v.tensor_copy(out=mid_bc[:], in_=ps_f[:, 0:32]))
            v.wait_ge(sv.h, sv.n)
            sv.inc(v.tensor_tensor(out=cumS_T[:], in0=cum_T[:],
                                   in1=mid_bc[:], op=ALU.subtract))
        p.V(_)
        cums_done = sv.mark()
        psf_free = cums_done

        def _(sc, cums_done=cums_done):
            wait(sc, cums_done)
            sc.activation(out=En[:], in_=cumS_T[:], func=AF.Exp, scale=-1.0)
            sc.activation(out=Ep[:], in_=cumS_T[:], func=AF.Exp)
            if debug and c == 0:
                sc.activation(out=snap1[:], in_=cumS_T[:], func=AF.Identity)
                sc.activation(out=snap2[:], in_=En[:], func=AF.Identity)
            sc.activation(out=r_tmp[:], in_=r_127[:], func=AF.Exp)
            ss.inc(sc.activation(out=r_127[:], in_=r_mid[:], func=AF.Exp))
        p.S(_)
        s_exps = ss.mark()

        def _(v, s_exps=s_exps):
            wait(v, s_exps)
            v.wait_ge(sv.h, sv.n)
            sv.inc(v.tensor_tensor(out=w2T[:], in0=En[:], in1=dtfT[:, 0:32],
                                   op=ALU.mult))
        p.V(_)
        w2_done = sv.mark()

        def _(t, w2_done=w2_done):
            wait(t, w2_done)
            wait(t, psf_free)
            st.inc(t.matmul(ps_f[0:1, 0:32], onehot[:], cumS_T[:], start=True,
                            stop=True))
        p.T(_)
        tmk = st.mark()

        def _(v, tmk=tmk):
            wait(v, tmk)
            sv.inc(v.tensor_copy(out=r_mid[:], in_=ps_f[0:1, 0:32]))
        p.V(_)
        esl_raw = sv.mark()
        psf_free = esl_raw

        def _(sc, esl_raw=esl_raw):
            wait(sc, esl_raw)
            ss.inc(sc.activation(out=r_esl[:], in_=r_mid[:], func=AF.Exp))
        p.S(_)
        s_esl = ss.mark()

        def _(t, s_esl=s_esl):
            wait(t, s_esl)
            wait(t, psf_free)
            t.matmul(ps_f[:, 0:32], ones_row[:], r_tmp[:], start=True,
                     stop=True)
            st.inc(t.matmul(ps_f[:, 32:64], ones_row[:], r_127[:], start=True,
                            stop=True))
        p.T(_)
        tmk = st.mark()

        def _(v, tmk=tmk):
            wait(v, tmk)
            v.tensor_copy(out=Atot_bc[:], in_=ps_f[:, 0:32])
            sv.inc(v.tensor_copy(out=eS_bc[:], in_=ps_f[:, 32:64]))
        p.V(_)
        bc1 = sv.mark()
        psf_free = bc1

        def _(t, bc1=bc1):
            wait(t, bc1)
            st.inc(t.matmul(ps_f[:, 0:32], ones_row[:], r_esl[:], start=True,
                            stop=True))
        p.T(_)
        tmk = st.mark()

        def _(v, tmk=tmk):
            wait(v, tmk)
            sv.inc(v.tensor_copy(out=esl_bc[:], in_=ps_f[:, 0:32]))
        p.V(_)
        bc2 = sv.mark()
        psf_free = bc2

        def _(t, c=c, cj=cj, bc2=bc2):
            wait(t, bc2)
            wait(t, pscb_free)
            st.inc(t.matmul(ps_cb[:], xbca_c[cj][:, 16 * 128:17 * 128],
                            xbca_c[cj][:, 17 * 128:18 * 128], start=True,
                            stop=True))
        p.T(_)
        tmk = st.mark()

        def _(v, tmk=tmk):
            wait(v, tmk)
            sv.inc(v.tensor_tensor(out=CBm[:], in0=ps_cb[:], in1=maskji[:],
                                   op=ALU.mult))
        p.V(_)
        cbm_done = sv.mark()
        pscb_free = cbm_done

        if debug and c == 0:
            def _(v, cbm_done=cbm_done):
                wait(v, cbm_done)
                sv.inc(v.tensor_copy(out=snap3[:], in_=cumS_T[:]))
            p.V(_)
            cbm_done = sv.mark()

        pc = prev[c % 2]
        pn = prev[(c + 1) % 2]
        vh_done = cbm_done
        for h in range(32):
            hj = h % 2

            def _(v, h=h, hj=hj, vh_done=vh_done):
                wait(v, vh_done)
                wait(v, psy_free[hj])
                wait(v, psst_free[hj])
                v.tensor_scalar(out=m_scr[:], in0=CBm[:],
                                scalar1=w2T[:, h:h + 1], scalar2=None,
                                op0=ALU.mult)
                v.tensor_scalar(out=prevS[:], in0=pc[:, h * 64:(h + 1) * 64],
                                scalar1=eS_bc[:, h:h + 1], scalar2=None,
                                op0=ALU.mult)
                sv.inc(v.tensor_scalar(out=wB[:], in0=B_tok[:],
                                       scalar1=w2T[:, h:h + 1], scalar2=None,
                                       op0=ALU.mult))
            p.V(_)
            v_in = sv.mark()

            def _(t, h=h, hj=hj, cj=cj, v_in=v_in, c=c):
                wait(t, v_in)
                if h == 0:
                    wait(t, x_tok_done)
                t.matmul(ps_y[hj][:], m_scr[:],
                         X_tok[:, h * 64:(h + 1) * 64], start=True, stop=False)
                t.matmul(ps_y[hj][:], xbca_c[cj][:, 17 * 128:18 * 128],
                         prevS[:], start=False, stop=True)
                st.inc(t.matmul(ps_st[hj][:], wB[:],
                                X_tok[:, h * 64:(h + 1) * 64], start=True,
                                stop=True))
            p.T(_)
            t_mm = st.mark()

            def _(v, h=h, hj=hj, t_mm=t_mm, pc=pc, pn=pn):
                wait(v, t_mm)
                v.tensor_scalar(out=y_scr[:], in0=ps_y[hj][:],
                                scalar1=Ep[:, h:h + 1], scalar2=None,
                                op0=ALU.mult)
                sv.inc(v.tensor_scalar(out=x_scr[:],
                                       in0=X_tok[:, h * 64:(h + 1) * 64],
                                       scalar1=D_bc[:, h:h + 1], scalar2=None,
                                       op0=ALU.mult))
                v.wait_ge(sv.h, sv.n)
                v.tensor_tensor(out=y_chunk[:, h * 64:(h + 1) * 64],
                                in0=y_scr[:], in1=x_scr[:], op=ALU.add)
                sv.inc(v.tensor_scalar(out=c_scr[:], in0=ps_st[hj][:],
                                       scalar1=esl_bc[:, h:h + 1], scalar2=None,
                                       op0=ALU.mult))
                v.wait_ge(sv.h, sv.n)
                sv.inc(v.scalar_tensor_tensor(
                    out=pn[:, h * 64:(h + 1) * 64],
                    in0=pc[:, h * 64:(h + 1) * 64],
                    scalar=Atot_bc[:, h:h + 1], in1=c_scr[:], op0=ALU.mult,
                    op1=ALU.add))
            p.V(_)
            vh_done = sv.mark()
            psy_free[hj] = vh_done
            psst_free[hj] = vh_done
        heads_done = sv.mark()
        chunk_done[cj] = heads_done

        if debug and c == 0:
            dbg = {}
            for nm, buf, w in [("d_xbca", xbca_c[0], NT * 128),
                               ("d_xtok", X_tok, 16 * 128),
                               ("d_btok", B_tok, 128),
                               ("d_cum", cum_T, 32), ("d_cums", cumS_T, 32),
                               ("d_en", En, 32), ("d_ep", Ep, 32),
                               ("d_w2", w2T, 32), ("d_cbm", CBm, 128),
                               ("d_atot", Atot_bc, 32), ("d_esl", esl_bc, 32),
                               ("d_es", eS_bc, 32), ("d_ych", y_chunk, 2048),
                               ("d_dtf", dtf, T), ("d_dta", dtA, T),
                               ("d_prev1", prev[1], 2048),
                               ("d_s1", snap1, 32), ("d_s2", snap2, 32),
                               ("d_s3", snap3, 32)]:
                dbg[nm] = p.dram_out(nm, (128, w), F32)

            def _(g, dbg=dbg):
                wait(g, heads_done)
                for (nm, buf, w) in [("d_xbca", xbca_c[0], NT * 128),
                                     ("d_xtok", X_tok, 16 * 128),
                                     ("d_btok", B_tok, 128),
                                     ("d_cum", cum_T, 32), ("d_cums", cumS_T, 32),
                                     ("d_en", En, 32), ("d_ep", Ep, 32),
                                     ("d_w2", w2T, 32), ("d_cbm", CBm, 128),
                                     ("d_atot", Atot_bc, 32), ("d_esl", esl_bc, 32),
                                     ("d_es", eS_bc, 32), ("d_ych", y_chunk, 2048),
                                     ("d_dtf", dtf, T), ("d_dta", dtA, T),
                                     ("d_prev1", prev[1], 2048),
                                     ("d_s1", snap1, 32), ("d_s2", snap2, 32),
                                     ("d_s3", snap3, 32)]:
                    so.inc(g.dma_start(out=dbg[nm][:, 0:w], in_=buf[:, 0:w]), 16)
                g.wait_ge(so.h, so.n)
            p.G(_)
            return p

        def _(sc, cj=cj, zl=zf_ld[cj]):
            wait(sc, zl)
            wait(sc, heads_done)
            ss.inc(sc.activation(out=zsig[:], in_=zf[cj][:], func=AF.Sigmoid))
        p.S(_)
        s_sig2 = ss.mark()

        def _(v, cj=cj, c=c, s_sig2=s_sig2):
            wait(v, s_sig2)
            wait(v, tst_used[cj])
            v.tensor_tensor(out=z2[:, 0:2048], in0=zf[cj][:], in1=zsig[:, 0:2048],
                            op=ALU.mult)
            sv.inc(v.tensor_tensor(out=t_stage[cj], in0=z2[:, 0:2048],
                                   in1=y_chunk[:], op=ALU.mult))
        p.V(_)
        t_done = sv.mark()
        zf_used[cj] = t_done

        def _(sc, cj=cj, c=c, t_done=t_done):
            wait(sc, t_done)
            ss.inc(sc.activation(out=z2[:, 0:2048], in_=t_stage[cj],
                                 func=AF.Square,
                                 accum_out=psq_sb[:, c:c + 1]))
        p.S(_)
        sq_done = ss.mark()

        def _(g, cj=cj, c=c, sq_done=sq_done):
            wait(g, sq_done)
            so.inc(g.dma_start(out=t_tok_o[c * 128:(c + 1) * 128, :],
                               in_=t_stage[cj]), 16)
        p.G(_)
        tst_used[cj] = so.mark()
        tst_st[cj] = so.mark()

    def _(g):
        wait(g, (ss, ss.n))
        so.inc(g.dma_start(out=psq_o[:], in_=psq_sb[:]), 16)
    p.G(_)
    fin = so.n

    def _(yq):
        yq.wait_ge(so.h, fin)
    p.Y(_)
    return p


def prep_launch2_inputs(inp, z_full, xbc_full, dt_full):
    iu = np.triu(np.ones((128, 128), dtype=np.float32))
    maskji = np.triu(np.ones((128, 128), dtype=np.float32))
    onehot = np.zeros((128, 1), dtype=np.float32)
    onehot[127, 0] = 1.0
    ident = np.eye(128, dtype=np.float32)
    maps = []
    for core in range(8):
        b, s = core // 2, core % 2
        conv_w = np.asarray(inp["L1_attn_conv_w"])
        conv_b = np.asarray(inp["L1_attn_conv_b"])
        rows = np.concatenate([np.arange(2048 * s, 2048 * s + 2048),
                               np.arange(4096, 4352)])
        cw = conv_w[rows]
        cb = conv_b[rows]
        m = dict(
            xbc_in=np.ascontiguousarray(xbc_full[b][rows]),
            zh_in=np.ascontiguousarray(
                z_full[b][2048 * s:2048 * s + 2048].T),
            dt_in=np.ascontiguousarray(dt_full[b][32 * s:32 * s + 32]),
            convw_in=np.ascontiguousarray(
                cw.reshape(18, 128, 4).transpose(1, 0, 2).reshape(128, 72)),
            convb_in=np.ascontiguousarray(
                cb.reshape(18, 128).T),
            alog_in=np.ascontiguousarray(
                np.asarray(inp["L1_attn_A_log"])[32 * s:32 * s + 32, None]),
            dtb_in=np.ascontiguousarray(
                np.asarray(inp["L1_attn_dt_bias"])[32 * s:32 * s + 32, None]),
            d_in=np.ascontiguousarray(
                np.asarray(inp["L1_attn_D"])[None, 32 * s:32 * s + 32]),
            triu_in=iu, maskji_in=maskji, ident_in2=ident, onehot_in=onehot,
        )
        maps.append(m)
    return maps



def build_launch3(debug=False):
    p = Prog()
    NW1 = 44
    PIECES = 44

    tT_in = p.dram_in("tT_in", (DIN, TL))
    sqa_in = p.dram_in("sqa_in", (1, TL))
    sqb_in = p.dram_in("sqb_in", (1, TL))
    x2T_in = p.dram_in("x2T_in", (C, TL))
    normw_in = p.dram_in("normw_in", (128, NDIN))
    wout_in = p.dram_in("wout_in", (128, NCT * NDIN * 128))
    w1t_in = p.dram_in("w1t_in", (128, NW1 * NCT * 128))
    w2t_in = p.dram_in("w2t_in", (128, NW1 * NCT * 128))
    w3t_in = p.dram_in("w3t_in", (128, NCT * NW1 * 128))

    out_o = p.dram_out("out_o", (C, TL), F32)

    q1d = p.nc.dram_tensor("q1d", [128, NW1 * NCT * 128], BF16,
                           kind="Internal").ap()
    q2d = p.nc.dram_tensor("q2d", [128, NW1 * NCT * 128], BF16,
                           kind="Internal").ap()
    q3d = p.nc.dram_tensor("q3d", [128, NCT * NW1 * 128], BF16,
                           kind="Internal").ap()

    ld = p.sem("ld")
    so = p.sem("so")
    st = p.sem("st")
    sv = p.sem("sv")
    ss = p.sem("ss")

    ones_col = p.sb("ones_col", (128, 1), F32)
    ones_colb = p.sb("ones_colb", (128, 1), BF16)
    ones_row = p.sb("ones_row", (1, 128), F32)
    normw = p.sb("normw", (128, NDIN), F32)
    U44 = p.sb("U44", (128, 22528), BF16)
    tnb = p.sb("tnb", (128, NDIN * 512), BF16)
    x3T = p.sb("x3T", (128, NCT * TL), BF16)
    U22 = p.sb("U22", (128, 11264), BF16)
    wq12 = [p.sb(f"wq12_{i}", (128, 4096), BF16) for i in range(2)]
    wf = [p.sb(f"wf{i}", (128, 2048), F32) for i in range(2)]
    q_scr = p.sb("q_scr", (128, 2048), F32)
    q_scr2 = p.sb("q_scr2", (128, 2048), F32)
    qp = p.sb("qp", (128, 2048), BF16)
    e_scr = p.sb("e_scr", (128, 512), F32)
    g_scr = p.sb("g_scr", (128, 512), F32)
    sqscr0 = p.sb("sqscr0", (128, 512), F32)
    sqscr = [sqscr0, sqscr0]
    acc = p.sb("acc", (128, 1), F32)
    racc = p.sb("racc", (128, 1), F32)
    s_col = [p.sb(f"s_col{i}", (128, 1), F32) for i in range(3)]
    srow = p.sb("srow", (1, 8), F32)
    out_st = [p.sb(f"out_st{i}", (128, 512), F32) for i in range(2)]
    sq_row = [p.sb(f"sq_row{i}", (1, TL), F32) for i in range(2)]
    st_mean = p.sb("st_mean", (1, 512), F32)
    st_var = p.sb("st_var", (1, 512), F32)
    st_rstd = st_var
    st_b = p.sb("st_b", (1, 512), F32)

    tb_sl = [U44[:, i * 512:(i + 1) * 512] for i in range(NDIN)]
    hid = U44[:, 0:NW1 * 512]
    x2b = U22[:, 0:NCT * 512]
    q3b = [U22[:, i * 5632:(i + 1) * 5632] for i in range(2)]

    ps_r = [p.ps(f"ps_r{i}", (128, 512)) for i in range(2)]
    ps_sum = p.ps("ps_sum", (1, 512))
    ps_sumsq = p.ps("ps_sumsq", (1, 512))
    ps_bc_a = p.ps("ps_bc_a", (128, 512))
    ps_bc_b = p.ps("ps_bc_b", (128, 512))

    sems = (st, sv, ss)
    ln_bufs = dict(ps_sum=ps_sum, ps_sumsq=ps_sumsq, ps_bc_a=ps_bc_a,
                   ps_bc_b=ps_bc_b, sq_scratch=sqscr, st_mean=st_mean,
                   st_var=st_var, st_rstd=st_rstd, st_b=st_b,
                   ones_col=ones_col, ones_row=ones_row,
                   ones_colb=ones_colb)
    ps_free = {0: None, 1: None}
    wq12_free = {0: None, 1: None}
    pctr = [0]

    p.begin()

    def _(v):
        v.memset(ones_col[:], 1.0)
        v.memset(ones_colb[:], 1.0)
        v.memset(acc[:], 0.0)
        sv.inc(v.memset(ones_row[:], 1.0))
    p.V(_)
    init_done = sv.mark()

    def _(g):
        wait(g, init_done)
        ld.inc(g.dma_start(out=normw[:], in_=normw_in[:]), 16)
        ld.inc(g.dma_start(out=sq_row[0][:], in_=sqa_in[:]), 16)
        ld.inc(g.dma_start(out=sq_row[1][:], in_=sqb_in[:]), 16)
    p.G(_)
    ld_c = ld.n

    wsrc = [w1t_in, w2t_in, w3t_in]
    qdst = [q1d, q2d, q3d]
    wf_free = {0: None, 1: None}
    for mi in range(3):
        def _(v, mi=mi):
            sv.inc(v.memset(acc[:], 0.0))
        p.V(_)
        for k in range(PIECES):
            bj = k % 2

            def _(g, mi=mi, k=k, bj=bj):
                wait(g, wf_free[bj])
                ld.inc(g.dma_start(out=wf[bj][:],
                                   in_=wsrc[mi][:, k * 2048:(k + 1) * 2048]),
                       16)
            p.G(_)
            ldk = ld.mark()

            def _(v, bj=bj, ldk=ldk):
                v.wait_ge(ldk[0].h, ldk[1])
                sv.inc(v.tensor_reduce(out=racc[:], in_=wf[bj][:],
                                       axis=mybir.AxisListType.X, op=ALU.add,
                                       apply_absolute_value=True))
                v.wait_ge(sv.h, sv.n)
                sv.inc(v.tensor_tensor(out=acc[:], in0=acc[:], in1=racc[:],
                                       op=ALU.add))
            p.V(_)
            wf_free[bj] = sv.mark()
        acc_done = sv.mark()

        def _(t, acc_done=acc_done):
            wait(t, acc_done)
            st.inc(t.matmul(ps_sum[:, 0:1], ones_col[:], acc[:], start=True,
                            stop=True))
        p.T(_)
        tm = st.mark()

        def _(v, mi=mi, tm=tm):
            wait(v, tm)
            sv.inc(v.tensor_scalar(out=srow[:, mi:mi + 1],
                                   in0=ps_sum[0:1, 0:1],
                                   scalar1=1.0 / (H1 * C), scalar2=BIT_EPS,
                                   op0=ALU.mult, op1=ALU.add))
            v.wait_ge(sv.h, sv.n)
            sv.inc(v.reciprocal(out=srow[:, 4 + mi:5 + mi],
                                in_=srow[:, mi:mi + 1]))
        p.V(_)
        s_done = sv.mark()

        def _(t, mi=mi, s_done=s_done):
            wait(t, s_done)
            st.inc(t.matmul(ps_bc_a[:, 0:1], ones_row[:],
                            srow[:, 4 + mi:5 + mi], start=True, stop=True))
        p.T(_)
        tm = st.mark()

        def _(v, mi=mi, tm=tm):
            wait(v, tm)
            sv.inc(v.tensor_copy(out=s_col[mi][:], in_=ps_bc_a[:, 0:1]))
        p.V(_)
        scol_done = sv.mark()

        qst_free = None
        for k in range(PIECES):
            bj = k % 2

            def _(g, mi=mi, k=k, bj=bj, scol_done=scol_done):
                wait(g, wf_free[bj])
                if k == 0:
                    wait(g, scol_done)
                ld.inc(g.dma_start(out=wf[bj][:],
                                   in_=wsrc[mi][:, k * 2048:(k + 1) * 2048]),
                       16)
            p.G(_)
            ldk = ld.mark()

            def _(v, mi=mi, bj=bj, ldk=ldk, qst_free=qst_free):
                v.wait_ge(ldk[0].h, ldk[1])
                wait(v, qst_free)
                v.tensor_scalar(out=q_scr[:], in0=wf[bj][:],
                                scalar1=s_col[mi][:, 0:1], scalar2=-1.0,
                                op0=ALU.mult, op1=ALU.max)
                v.tensor_scalar(out=q_scr2[:], in0=q_scr[:], scalar1=1.0,
                                scalar2=MAGIC, op0=ALU.min, op1=ALU.add)
                sv.inc(v.tensor_scalar(out=qp[:], in0=q_scr2[:],
                                       scalar1=MAGIC, op0=ALU.subtract,
                                       scalar2=None))
            p.V(_)
            wf_free[bj] = sv.mark()
            qev = sv.mark()

            def _(g, mi=mi, k=k, qev=qev):
                wait(g, qev)
                so.inc(g.dma_start(out=qdst[mi][:, k * 2048:(k + 1) * 2048],
                                   in_=qp[:]), 16)
            p.G(_)
            qst_free = so.mark()
    quant_all = so.n

    for tb in range(2):
        ld_guard = sv.mark()

        def _(g, tb=tb, ld_guard=ld_guard):
            if tb == 0:
                g.wait_ge(ld.h, ld_c)
            else:
                wait(g, ld_guard)
            for i in range(NDIN):
                ld.inc(g.dma_start(
                    out=tb_sl[i],
                    in_=tT_in[i * 128:(i + 1) * 128,
                              tb * 512: tb * 512 + 512]), 16)
            for i in range(NCT):
                ld.inc(g.dma_start(
                    out=x2b[:, i * 512:(i + 1) * 512],
                    in_=x2T_in[i * 128:(i + 1) * 128,
                               tb * 512: tb * 512 + 512]), 16)
        p.G(_)
        ld_t = ld.mark()

        def _(v, tb=tb):
            v.wait_ge(ld.h, ld_c)
            sv.inc(v.tensor_tensor(out=st_mean[:],
                                   in0=sq_row[0][:, tb * 512: tb * 512 + 512],
                                   in1=sq_row[1][:, tb * 512: tb * 512 + 512],
                                   op=ALU.add))
            v.wait_ge(sv.h, sv.n)
            sv.inc(v.tensor_scalar(out=st_var[:], in0=st_mean[:],
                                   scalar1=1.0 / DIN, scalar2=RMS_EPS,
                                   op0=ALU.mult, op1=ALU.add))
        p.V(_)
        v_rs = sv.mark()

        def _(sc, v_rs=v_rs):
            wait(sc, v_rs)
            sc.activation(out=st_b[:], in_=st_var[:], func=AF.Ln)
            ss.inc(sc.activation(out=st_var[:], in_=st_b[:], func=AF.Exp,
                                 scale=-0.5))
        p.S(_)
        s_rstd = ss.mark()

        def _(t, s_rstd=s_rstd, tb=tb):
            wait(t, s_rstd)
            wait(t, ln_bufs.get("free", {}).get("bc"))
            st.inc(t.matmul(ps_bc_a[:], ones_row[:], st_var[:], start=True,
                            stop=True))
        p.T(_)
        t_bc = st.mark()

        def _(v, tb=tb, t_bc=t_bc, ld_t=ld_t):
            wait(v, t_bc)
            v.wait_ge(ld_t[0].h, ld_t[1])
            for i in range(NDIN):
                v.tensor_copy(out=sqscr0[:], in_=tb_sl[i])
                v.tensor_tensor(out=sqscr0[:], in0=sqscr0[:], in1=ps_bc_a[:],
                                op=ALU.mult)
                sv.inc(v.tensor_scalar(out=tnb[:, i * 512:(i + 1) * 512],
                                       in0=sqscr0[:],
                                       scalar1=normw[:, i:i + 1],
                                       scalar2=None, op0=ALU.mult))
        p.V(_)
        tnb_done = sv.mark()
        ln_bufs.setdefault("free", {})["bc"] = tnb_done

        for i in range(NCT):
            bj = pctr[0] % 2

            def _(g, i=i, bj=bj):
                wait(g, wq12_free[bj])
                ld.inc(g.dma_start(
                    out=wq12[bj][:],
                    in_=wout_in[:, i * 4096:(i + 1) * 4096]), 16)
            p.G(_)
            ld_w = ld.mark()

            pi = pctr[0] % 2
            pctr[0] += 1

            def _(t, i=i, pi=pi, bj=bj, ld_w=ld_w, tnb_done=tnb_done):
                t.wait_ge(ld_w[0].h, ld_w[1])
                wait(t, ps_free[pi])
                if i == 0:
                    wait(t, tnb_done)
                for ci in range(NDIN):
                    ins = t.matmul(ps_r[pi][:],
                                   wq12[bj][:, ci * 128:(ci + 1) * 128],
                                   tnb[:, ci * 512:(ci + 1) * 512],
                                   start=(ci == 0), stop=(ci == NDIN - 1))
                st.inc(ins)
            p.T(_)
            wq12_free[bj] = st.mark()
            mm = st.mark()

            def _(v, i=i, tb=tb, pi=pi, mm=mm):
                wait(v, mm)
                sv.inc(v.tensor_copy(out=e_scr[:], in_=ps_r[pi][:]))
                v.wait_ge(sv.h, sv.n)
                sv.inc(v.tensor_tensor(
                    out=x3T[:, i * TL + tb * 512: i * TL + tb * 512 + 512],
                    in0=e_scr[:], in1=x2b[:, i * 512:(i + 1) * 512],
                    op=ALU.add))
            p.V(_)
            ps_free[pi] = sv.mark()
        ln_emit(p, sems,
                [x3T[:, i * TL + tb * 512: i * TL + tb * 512 + 512]
                 for i in range(NCT)],
                (lambda i, tb=tb: x3T[:, i * TL + tb * 512:
                                      i * TL + tb * 512 + 512]), ln_bufs,
                ready=sv.mark())
        if debug == 1 and tb == 0:
            d_tnb = p.dram_out("d_tnb", (DIN, 512), F32)
            d_x3 = p.dram_out("d_x3", (C, 512), F32)
            dm = sv.mark()

            def _(g, dm=dm):
                g.wait_ge(sv.h, dm[1])
                for i in range(NDIN):
                    so.inc(g.dma_start(out=d_tnb[i * 128:(i + 1) * 128, :],
                                       in_=tnb[:, i * 512:(i + 1) * 512]), 16)
                for i in range(NCT):
                    so.inc(g.dma_start(out=d_x3[i * 128:(i + 1) * 128, :],
                                       in_=x3T[:, i * TL: i * TL + 512]), 16)
                g.wait_ge(so.h, so.n)
            p.G(_)
            return p
    x3_done = sv.n
    m_x3 = (sv, x3_done)

    q12_free = wq12_free
    q3_free = {0: None, 1: None}
    hid_war = None
    for tb in range(2):
        for fi in range(NW1):
            bj = pctr[0] % 2
            pctr[0] += 1

            def _(g, fi=fi, bj=bj):
                wait(g, q12_free[bj])
                if fi == 0 and tb == 0:
                    g.wait_ge(so.h, quant_all)
                ld.inc(g.dma_start(
                    out=wq12[bj][:, 0:2048],
                    in_=q1d[:, fi * 2048:(fi + 1) * 2048]), 16)
                ld.inc(g.dma_start(
                    out=wq12[bj][:, 2048:4096],
                    in_=q2d[:, fi * 2048:(fi + 1) * 2048]), 16)
            p.G(_)
            ld_q = ld.mark()

            pi1 = 0
            pi2 = 1

            def _(t, fi=fi, tb=tb, bj=bj, ld_q=ld_q):
                t.wait_ge(ld_q[0].h, ld_q[1])
                wait(t, ps_free[0])
                wait(t, ps_free[1])
                if fi == 0 and tb == 0:
                    wait(t, m_x3)
                for ci in range(NCT):
                    ins = t.matmul(
                        ps_r[0][:], wq12[bj][:, ci * 128:(ci + 1) * 128],
                        x3T[:, ci * TL + tb * 512: ci * TL + tb * 512 + 512],
                        start=(ci == 0), stop=(ci == NCT - 1))
                for ci in range(NCT):
                    ins = t.matmul(
                        ps_r[1][:],
                        wq12[bj][:, 2048 + ci * 128: 2048 + (ci + 1) * 128],
                        x3T[:, ci * TL + tb * 512: ci * TL + tb * 512 + 512],
                        start=(ci == 0), stop=(ci == NCT - 1))
                st.inc(ins)
            p.T(_)
            q12_free[bj] = st.mark()
            mm = st.mark()

            def _(sc, mm=mm, hid_war=hid_war):
                wait(sc, mm)
                ss.inc(sc.activation(out=g_scr[:], in_=ps_r[0][:],
                                     func=AF.Sigmoid))
            p.S(_)
            s_sig = ss.mark()

            def _(v, fi=fi, s_sig=s_sig, hid_war=hid_war):
                wait(v, s_sig)
                wait(v, hid_war)
                sv.inc(v.tensor_copy(out=e_scr[:], in_=ps_r[1][:]))
                v.wait_ge(sv.h, sv.n)
                sv.inc(v.tensor_tensor(out=hid[:, fi * 512:(fi + 1) * 512],
                                       in0=e_scr[:], in1=g_scr[:],
                                       op=ALU.mult))
            p.V(_)
            ps_free[0] = sv.mark()
            ps_free[1] = sv.mark()
        hid_done = sv.mark()

        if debug == 2 and tb == 0:
            d_hid = p.dram_out("d_hid", (H1, 512), F32)
            d_srow = p.dram_out("d_srow", (1, 8), F32)

            def _(g, hid_done=hid_done):
                g.wait_ge(sv.h, hid_done[1])
                for i in range(NW1):
                    so.inc(g.dma_start(out=d_hid[i * 128:(i + 1) * 128, :],
                                       in_=hid[:, i * 512:(i + 1) * 512]), 16)
                so.inc(g.dma_start(out=d_srow[:], in_=srow[:]), 16)
                g.wait_ge(so.h, so.n)
            p.G(_)
            return p

        for i in range(NCT):
            bj = pctr[0] % 2
            pctr[0] += 1

            def _(g, i=i, bj=bj, tb=tb):
                wait(g, q3_free[bj])
                if i == 0 and tb == 0:
                    wait(g, (sv, x3_done))
                ld.inc(g.dma_start(out=q3b[bj],
                                   in_=q3d[:, i * 5632:(i + 1) * 5632]), 16)
            p.G(_)
            ld_q3 = ld.mark()

            pi = pctr[0] % 2

            def _(t, i=i, pi=pi, bj=bj, ld_q3=ld_q3, hid_done=hid_done):
                t.wait_ge(ld_q3[0].h, ld_q3[1])
                wait(t, ps_free[pi])
                if i == 0:
                    wait(t, hid_done)
                for ci in range(NW1):
                    ins = t.matmul(ps_r[pi][:],
                                   q3b[bj][:, ci * 128:(ci + 1) * 128],
                                   hid[:, ci * 512:(ci + 1) * 512],
                                   start=(ci == 0), stop=(ci == NW1 - 1))
                st.inc(ins)
            p.T(_)
            q3_free[bj] = st.mark()
            mm = st.mark()

            def _(v, i=i, tb=tb, pi=pi, mm=mm):
                wait(v, mm)
                sv.inc(v.tensor_copy(out=e_scr[:], in_=ps_r[pi][:]))
                v.wait_ge(sv.h, sv.n)
                sv.inc(v.tensor_tensor(
                    out=tnb[:, i * 512:(i + 1) * 512], in0=e_scr[:],
                    in1=x3T[:, i * TL + tb * 512: i * TL + tb * 512 + 512],
                    op=ALU.add))
            p.V(_)
            ps_free[pi] = sv.mark()
        y_done = sv.mark()
        hid_war = y_done

        if debug == 3 and tb == 0:
            d_y3 = p.dram_out("d_y3", (C, 512), F32)

            def _(g, y_done=y_done):
                g.wait_ge(sv.h, y_done[1])
                for i in range(NCT):
                    so.inc(g.dma_start(out=d_y3[i * 128:(i + 1) * 128, :],
                                       in_=tnb[:, i * 512:(i + 1) * 512]), 16)
                g.wait_ge(so.h, so.n)
            p.G(_)
            return p

        ln_done = ln_emit(
            p, sems,
            [tnb[:, i * 512:(i + 1) * 512] for i in range(NCT)],
            (lambda i, tb=tb: x3T[:, i * TL + tb * 512:
                                  i * TL + tb * 512 + 512]),
            ln_bufs, ready=sv.mark())

        def _(g, tb=tb, ln_done=ln_done):
            g.wait_ge(sv.h, ln_done)
            for i in range(NCT):
                so.inc(g.dma_start(
                    out=out_o[i * 128:(i + 1) * 128, tb * 512: tb * 512 + 512],
                    in_=x3T[:, i * TL + tb * 512: i * TL + tb * 512 + 512]),
                    16)
        p.G(_)
        out_stored = so.mark()
    fin = so.n

    def _(yq):
        yq.wait_ge(so.h, fin)
    p.Y(_)
    return p


def prep_launch3_inputs(inp, t_full, psq_cores, x2_full):
    w1 = np.asarray(inp["L1_ffn_w1"])
    w2 = np.asarray(inp["L1_ffn_w2"])
    w3 = np.asarray(inp["L1_ffn_w3"])
    wout = np.asarray(inp["L1_attn_W_out"])
    normw = np.asarray(inp["L1_attn_norm_w"])
    shared = dict(
        normw_in=_cols(normw),
        wout_in=_slabc(wout),
        w1t_in=_slabc(np.ascontiguousarray(w1.T)),
        w2t_in=_slabc(np.ascontiguousarray(w2.T)),
        w3t_in=_slabc(np.ascontiguousarray(w3.T)),
    )
    maps = []
    for core in range(8):
        b, s = core // 2, core % 2
        tok = slice(1024 * s, 1024 * s + 1024)
        sq = [np.ascontiguousarray(
            psq_cores[2 * b + half].T.reshape(1, 2048)[:, tok])
            for half in range(2)]
        m = dict(shared)
        m["tT_in"] = np.ascontiguousarray(t_full[b, tok].T)
        m["sqa_in"] = sq[0]
        m["sqb_in"] = sq[1]
        m["x2T_in"] = np.ascontiguousarray(x2_full[b][:, tok])
        maps.append(m)
    return maps


_PROGS = {}


def _get_progs():
    if not _PROGS:
        _PROGS["l1e"] = build_launch1(0)
        _PROGS["l1o"] = build_launch1(1)
        _PROGS["l2"] = build_launch2()
        _PROGS["l3"] = build_launch3()
    return _PROGS


def kernel(**inputs):
    progs = _get_progs()
    maps1 = prep_launch1_inputs(inputs)

    re = run_bass_kernel_spmd(progs["l1e"].nc, [maps1[c] for c in (0, 2, 4, 6)],
                              core_ids=[0, 1, 2, 3])
    ro = run_bass_kernel_spmd(progs["l1o"].nc, [maps1[c] for c in (1, 3, 5, 7)],
                              core_ids=[0, 1, 2, 3])

    def gather1(key, nrows):
        full = np.zeros((B, nrows, T), np.float32)
        for j, c in enumerate((0, 2, 4, 6)):
            full[c // 2, :, 0:TL] = np.asarray(re.results[j][key])
        for j, c in enumerate((1, 3, 5, 7)):
            full[c // 2, :, TL:T] = np.asarray(ro.results[j][key])
        return full

    x2_full = gather1("x2T_o", C)
    z_full = gather1("zT_o", DIN)
    xbc_full = gather1("xbcT_o", XBC)
    dt_full = gather1("dt_o", NH)

    maps2 = prep_launch2_inputs(inputs, z_full, xbc_full, dt_full)
    r2 = run_bass_kernel_spmd(progs["l2"].nc, maps2, core_ids=list(range(8)))
    t_full = np.zeros((B, T, DIN), np.float32)
    psq_cores = []
    for core in range(8):
        b, s = core // 2, core % 2
        t_full[b, :, 2048 * s:2048 * s + 2048] = np.asarray(
            r2.results[core]["t_tok_o"])
        psq_cores.append(np.asarray(r2.results[core]["psq_o"]))

    maps3 = prep_launch3_inputs(inputs, t_full, psq_cores, x2_full)
    r3 = run_bass_kernel_spmd(progs["l3"].nc, maps3, core_ids=list(range(8)))
    out = np.zeros((B, T, C), np.float32)
    for core in range(8):
        b, s = core // 2, core % 2
        out[b, TL * s:TL * s + TL, :] = np.asarray(r3.results[core]["out_o"]).T
    return out
